# version 01 (final)
# baseline (speedup 1.0000x reference)
import numpy as np
import ml_dtypes
from contextlib import ExitStack
import concourse.bass as bass
import concourse.mybir as mybir
from concourse.bass_utils import run_bass_kernel_spmd

F32 = mybir.dt.float32
BF16 = mybir.dt.bfloat16
U8 = mybir.dt.uint8
AF = mybir.ActivationFunctionType
ALU = mybir.AluOpType
AX = mybir.AxisListType

NCORE = 8
D = 2048
T = 1024
H = 16
NCH = 8
DC = 16
IC = 8
EPS_ATTN = 1e-6
EPS_LN = 1e-5

DEBUG_TAPS = {}

ENG_ATTR = {'pe': 'tensor', 've': 'vector', 'se': 'scalar',
            'sy': 'sync', 'gp': 'gpsimd'}


class E:

    def __init__(self, nc, t, sems, engine, marks):
        self.nc = nc
        self.t = t
        self.sems = sems
        self.engine = engine
        self.counting = engine is None
        self.marks = marks
        self.n = {k: 0 for k in sems}

    def mk(self, key):
        if self.counting:
            assert key not in self.marks, f"duplicate mark {key}"
            self.marks[key] = dict(self.n)

    def wait(self, eng, sem, value):
        if self.engine == eng and value > 0:
            getattr(self.nc, ENG_ATTR[eng]).wait_ge(self.sems[sem], value)

    def wait_mark(self, eng, key, sem):
        if self.engine == eng:
            v = self.marks[key][sem]
            if v > 0:
                getattr(self.nc, ENG_ATTR[eng]).wait_ge(self.sems[sem], v)

    def mm(self, out, lhsT, rhs, start, stop, inc=0):
        if self.engine == 'pe':
            i = self.nc.tensor.matmul(out, lhsT, rhs, start=start, stop=stop)
            if inc:
                i.then_inc(self.sems['pe'], inc)
        self.n['pe'] += inc

    def trans(self, out, in_, inc=0):
        if self.engine == 'pe':
            i = self.nc.tensor.transpose(out, in_, self.t.ident[:, :])
            if inc:
                i.then_inc(self.sems['pe'], inc)
        self.n['pe'] += inc

    def act(self, out, in_, func, bias=0.0, scale=1.0, accum_out=None, inc=0,
            fenced=False):
        if self.engine == 'se':
            i = self.nc.scalar.activation(out, in_, func, bias=bias, scale=scale,
                                          accum_out=accum_out)
            if inc:
                if fenced:
                    self.nc.scalar.drain().then_inc(self.sems['se'], inc)
                else:
                    i.then_inc(self.sems['se'], inc)
        self.n['se'] += inc

    def _v(self, inc, f, fenced=False):
        if self.engine == 've':
            i = f(self.nc.vector)
            if inc:
                if fenced:
                    self.nc.vector.drain().then_inc(self.sems['ve'], inc)
                else:
                    i.then_inc(self.sems['ve'], inc)
        self.n['ve'] += inc

    def ve_tt(self, out, in0, in1, op, inc=0, fenced=False):
        self._v(inc, lambda v: v.tensor_tensor(out, in0, in1, op), fenced)

    def ve_ts(self, out, in0, s1, s2, op0, op1=None, inc=0, fenced=False):
        if op1 is None:
            self._v(inc, lambda v: v.tensor_scalar(out, in0, s1, None, op0), fenced)
        else:
            self._v(inc, lambda v: v.tensor_scalar(out, in0, s1, s2, op0, op1), fenced)

    def ve_stt(self, out, in0, scalar, in1, op0, op1, inc=0, fenced=False):
        self._v(inc, lambda v: v.scalar_tensor_tensor(out, in0, scalar, in1, op0, op1),
                fenced)

    def ve_copy(self, out, in_, inc=0, fenced=False):
        self._v(inc, lambda v: v.tensor_copy(out, in_), fenced)

    def ve_red(self, out, in_, op, negate=False, inc=0, fenced=False):
        self._v(inc, lambda v: v.tensor_reduce(out, in_, AX.X, op, negate=negate),
                fenced)

    def ve_recip(self, out, in_, inc=0, fenced=False):
        self._v(inc, lambda v: v.reciprocal(out, in_), fenced)

    def ve_memset(self, ap, val, inc=0):
        self._v(inc, lambda v: v.memset(ap, val))

    def ve_drain(self):
        if self.engine == 've':
            self.nc.vector.drain()

    def dma(self, queue, out, in_, sem=None, inc=16):
        sem = sem or ('dw' if queue == 'gp' else 'dx')
        if self.engine == queue:
            eng = self.nc.gpsimd if queue == 'gp' else self.nc.sync
            eng.dma_start(out=out, in_=in_).then_inc(self.sems[sem], inc)
        self.n[sem] += inc

    def collective(self, ins, outs):
        if self.engine == 'gp':
            self.nc.gpsimd.collective_compute(
                "AllGather", ALU.bypass,
                replica_groups=[list(range(NCORE))],
                ins=ins, outs=outs,
            ).then_inc(self.sems['cc'], 1)
        self.n['cc'] += 1

    def fence(self, eng, inc=1):
        if self.engine == eng:
            getattr(self.nc, ENG_ATTR[eng]).drain().then_inc(self.sems[eng], inc)
        self.n[eng] += inc

    def phase(self, name):
        for eng in ('pe', 've', 'se'):
            self.fence(eng)
        self.mk(('phase', name))
        for eng in ('pe', 've', 'se'):
            for other in ('pe', 've', 'se'):
                if other != eng:
                    self.wait_mark(eng, ('phase', name), other)


class NS:
    pass


def build_nc(stop_after=None):
    nc = bass.Bass("TRN2", target_bir_lowering=False)
    es = ExitStack()
    nc._keepalive = es

    dr = NS()
    dr.xT = nc.dram_tensor("xT", [D, T], F32, kind="ExternalInput")
    dr.xr = nc.dram_tensor("xr", [T, D], F32, kind="ExternalInput")
    dr.wq = nc.dram_tensor("wq", [D, D], F32, kind="ExternalInput")
    dr.wk = nc.dram_tensor("wk", [D, D], F32, kind="ExternalInput")
    dr.wv = nc.dram_tensor("wv", [D, D], F32, kind="ExternalInput")
    dr.wo = nc.dram_tensor("wo", [D, D], F32, kind="ExternalInput")
    dr.bqp = nc.dram_tensor("bqp", [DC, 128], F32, kind="ExternalInput")
    dr.bkp = nc.dram_tensor("bkp", [DC, 128], F32, kind="ExternalInput")
    dr.bvr = nc.dram_tensor("bvr", [1, D], F32, kind="ExternalInput")
    dr.bor = nc.dram_tensor("bor", [1, D], F32, kind="ExternalInput")
    dr.w1t = nc.dram_tensor("w1t", [8, D, 1024], F32, kind="ExternalInput")
    dr.w3t = nc.dram_tensor("w3t", [8, D, 1024], F32, kind="ExternalInput")
    dr.w2t = nc.dram_tensor("w2t", [8, 1024, D], F32, kind="ExternalInput")
    dr.ws1t = nc.dram_tensor("ws1t", [D, 1024], F32, kind="ExternalInput")
    dr.ws3t = nc.dram_tensor("ws3t", [D, 1024], F32, kind="ExternalInput")
    dr.ws2t = nc.dram_tensor("ws2t", [1024, D], F32, kind="ExternalInput")
    dr.gt = nc.dram_tensor("gt", [D, 8], F32, kind="ExternalInput")
    dr.identin = nc.dram_tensor("identin", [128, 128], BF16, kind="ExternalInput")
    dr.maskin = nc.dram_tensor("maskin", [128, 128], F32, kind="ExternalInput")
    dr.pmask = nc.dram_tensor("pmask", [128, 8], F32, kind="ExternalInput")
    dr.out = nc.dram_tensor("out", [T, D], F32, kind="ExternalOutput")
    dr.cc_in = nc.dram_tensor("cc_in", [D, 129], F32)
    dr.cc_out = nc.dram_tensor("cc_out", [NCORE * D, 129], F32, addr_space="Shared")
    dr.taps = {}
    for name, (_fn, shape, dt_) in DEBUG_TAPS.items():
        dr.taps[name] = nc.dram_tensor("tap_" + name, list(shape), dt_,
                                       kind="ExternalOutput")

    t = NS()
    al = nc.alloc_sbuf_tensor
    R1 = al("R1", [128, 32768], U8)
    R2 = al("R2", [128, 32768], U8)
    R3 = al("R3", [128, 32768], U8)
    RV = al("RV", [128, 33024], U8)
    RS = al("RS", [128, 20640], U8)
    RB = al("RB", [128, 8256], U8)

    def bfv(reg, off, *shape):
        n = int(np.prod(shape)) * 2
        ap = reg.ap()[:, off:off + n].bitcast(BF16)
        if len(shape) > 1:
            ap = ap.rearrange("p (a b) -> p a b", b=shape[-1])
        return ap

    def f32v(reg, off, *shape):
        n = int(np.prod(shape)) * 4
        ap = reg.ap()[:, off:off + n].bitcast(F32)
        if len(shape) > 1:
            ap = ap.rearrange("p (a b) -> p a b", b=shape[-1])
        return ap

    t.x_fm = bfv(R1, 0, DC, 1024)
    t.ao_fm = bfv(R1, 0, DC, 1024)
    t.w2b = bfv(R1, 0, IC, 2048)
    t.qf = bfv(R2, 0, DC, 1024)
    t.x1_fm = bfv(R2, 0, DC, 1024)
    t.kf = bfv(R3, 0, DC, 1024)
    t.g = bfv(R3, 0, IC, 1024)
    t.v = [bfv(RV, n * 4128, H, 129) for n in range(NCH)]
    t.S_tot = f32v(RS, 0, H, 129)
    t.S_run = f32v(RS, 8256, H, 129)
    t.S_bf = bfv(RS, 16512, H, 129)
    t.blk = f32v(RB, 0, H, 129)
    t.sq = f32v(RB, 0, 2048)
    t.y = ([f32v(RV, i * 8256, 2048) for i in range(4)] +
           [f32v(RS, i * 8256, 2048) for i in range(2)] +
           [f32v(R3, 16384 + i * 8192, 2048) for i in range(2)])

    t.vring = [al(f"vring{i}", [128, DC, 256], BF16) for i in range(2)]
    t.wring = [al(f"wring{i}", [128, DC, 128], BF16) for i in range(4)]

    t.ident = al("ident", [128, 128], BF16)
    t.mask = al("mask", [128, 128], F32)
    t.ones = al("ones", [1, 128], BF16)
    t.bqc = [al(f"bqc{i}", [128, 1], F32) for i in range(DC)]
    t.bkc = [al(f"bkc{i}", [128, 1], F32) for i in range(DC)]
    t.brow = al("brow_s", [1, D], BF16)
    t.bvr = t.brow
    t.bor = t.brow
    t.gt = al("gt_s", [128, DC, 8], BF16)
    t.pmask = al("pmask_s", [128, 8], F32)
    RX = al("RX", [128, 8192], U8)
    t.exs = [f32v(RX, i * 2048, 512) for i in range(2)]
    t.rs_ = [f32v(RX, 4096 + i * 2048, 512) for i in range(2)]
    t.xb = bfv(RX, 0, 2048)
    t.ktm = [al(f"ktm{i}", [128, 128], BF16) for i in range(2)]
    t.aosc = [al(f"aosc{i}", [128, 128], BF16) for i in range(2)]
    t.rcs = [al(f"rc{i}", [128, 1], F32) for i in range(2)]
    t.lg = al("lg", [128, 8], F32)
    t.comb = al("comb", [128, 64], F32)
    t.st1 = al("st1", [128, 1], F32)
    t.ssq = al("ssq", [128, 1], F32)
    t.mean = al("mean", [128, 1], F32)
    t.w_ = al("w_", [128, 1], F32)
    t.rstd = al("rstd", [128, 1], F32)
    t.gm1 = al("gm1", [128, 1], F32)
    t.gms = al("gms", [128, 4], F32)

    t.pb = [nc.alloc_psum_tensor(f"pb{i}", [128, 512], F32) for i in range(8)]

    def pbf(i):
        return t.pb[i].ap()[:, 0:64].bitcast(BF16)

    t.pbf = pbf

    sems = {}
    for s in ['pe', 've', 'se', 'dw', 'dx', 'cc', 'so']:
        sems[s] = es.enter_context(nc.semaphore(s))

    def wcol256(w, j):
        return w.ap()[:, j * 256:(j + 1) * 256].rearrange("(c p) j -> p c j", p=128)

    def wcol128(w, j):
        return w.ap()[:, j * 128:(j + 1) * 128].rearrange("(c p) j -> p c j", p=128)

    def w13_col(we, e, j):
        if e >= 0:
            base = dr.w1t if we == 1 else dr.w3t
            ap = base.ap()[e, :, j * 128:(j + 1) * 128]
        else:
            base = dr.ws1t if we == 1 else dr.ws3t
            ap = base.ap()[:, j * 128:(j + 1) * 128]
        return ap.rearrange("(c p) j -> p c j", p=128)

    def w2_all(e):
        ap = dr.w2t.ap()[e] if e >= 0 else dr.ws2t.ap()
        return ap.rearrange("(c p) j -> p c j", p=128)

    def program(EE):
        t_ = t

        def finish():
            EE.mk('gp_end')
            for name, (fn, _shape, _dt) in DEBUG_TAPS.items():
                for s_ in ('pe', 've', 'se'):
                    EE.wait('sy', s_, EE.n[s_])
                EE.dma('sy', dr.taps[name].ap(), fn(t_), sem='so')
            EE.wait('sy', 'so', EE.n['so'])

        EE.dma('sy', t_.ident[:, :], dr.identin[:, :])
        EE.dma('sy', t_.mask[:, :], dr.maskin[:, :])
        for c_ in range(DC):
            EE.dma('sy', t_.bqc[c_][:, :], dr.bqp.ap()[c_, :].unsqueeze(1))
            EE.dma('sy', t_.bkc[c_][:, :], dr.bkp.ap()[c_, :].unsqueeze(1))
        EE.dma('sy', t_.pmask[:, :], dr.pmask[:, :])
        EE.mk('consts')
        EE.dma('gp', t_.gt[:, :, :], dr.gt.ap().rearrange("(c p) j -> p c j", p=128))
        EE.dma('gp', t_.bvr[:, :], dr.bvr[:, :])
        EE.mk('smallw')
        for q in range(4):
            EE.dma('gp', t_.x_fm[:, :, q * 256:(q + 1) * 256],
                   dr.xT.ap()[:, q * 256:(q + 1) * 256].rearrange("(c p) t -> p c t", p=128))
            EE.mk(('xfm', q))
        EE.ve_memset(t_.ones[:, :], 1.0)
        EE.ve_memset(t_.S_run[:, :, :], 0.0)
        for n in range(NCH):
            EE.ve_memset(t_.v[n][:, :, 128:129], 1.0)
        EE.fence('ve')
        EE.mk('p0ve')

        def load_chunk(key, out_ap, in_ap, release_key):
            if release_key is not None:
                EE.wait_mark('gp', release_key, 'pe')
            EE.dma('gp', out_ap, in_ap)
            EE.mk(('dw', key))

        def pe_wait_chunk(key, next_key=None):
            EE.wait_mark('pe', ('dw', next_key if next_key is not None else key),
                         'dw')

        EE.mk('p0done')
        for j in range(8):
            load_chunk(('v', j), t_.vring[j % 2][:, :, :], wcol256(dr.wv, j),
                       ('rel_v', j - 2) if j >= 2 else None)
        for j in range(8):
            pe_wait_chunk(('v', j), ('v', j + 1) if j < 7 else None)
            for tau in range(NCH):
                gi = j * 8 + tau
                pb = t_.pb[gi % 4].ap()[:, 0:256]
                if gi >= 4:
                    EE.wait_mark('pe', ('vev', gi - 4), 'se')
                if j == 0:
                    q_ = tau // 2
                    EE.wait_mark('pe',
                                 ('xfm', q_ + 1) if q_ < 3 else ('dw', ('v', 0)),
                                 'dw')
                    EE.wait_mark('pe', 'p0ve', 've')
                EE.mm(pb, t_.ones[:, :], t_.bvr[:, j * 256:(j + 1) * 256], True, False)
                for dmc in range(DC):
                    EE.mm(pb, t_.x_fm[:, dmc, tau * 128:(tau + 1) * 128],
                          t_.vring[j % 2][:, dmc, :], False, dmc == DC - 1,
                          inc=1 if dmc == DC - 1 else 0)
                EE.mk(('pev', gi))
                EE.wait_mark('se', ('pev', gi), 'pe')
                for hh in range(2):
                    EE.act(t_.v[tau][:, 2 * j + hh, 0:128],
                           pb[:, hh * 128:(hh + 1) * 128], AF.Copy,
                           inc=1 if hh == 1 else 0, fenced=(hh == 1))
                EE.mk(('vev', gi))
            EE.mk(('rel_v', j))
        EE.phase('p1')
        if stop_after == 'p1':
            finish()
            return

        def proj_fm(tag, wdram, bias_sb, dst):
            for ofc in range(DC):
                load_chunk((tag, ofc), t_.wring[ofc % 4][:, :, :], wcol128(wdram, ofc),
                           ('rel_' + tag, ofc - 4) if ofc >= 4 else
                           (('rel_v', 7) if tag == 'k' and ofc == 0 else
                            (('rel_k', DC - 1) if tag == 'q' and ofc == 0 else None)))
            for ofc in range(DC):
                pe_wait_chunk((tag, ofc), (tag, ofc + 1) if ofc < DC - 1 else None)
                for tb in range(2):
                    gi = ofc * 2 + tb
                    pb = t_.pb[gi % 4].ap()[:, 0:512]
                    if gi >= 4:
                        EE.wait_mark('pe', (tag + 'se', gi - 4), 'se')
                    for dmc in range(DC):
                        EE.mm(pb, t_.wring[ofc % 4][:, dmc, :],
                              t_.x_fm[:, dmc, tb * 512:(tb + 1) * 512],
                              dmc == 0, dmc == DC - 1,
                              inc=1 if dmc == DC - 1 else 0)
                    EE.mk((tag + 'pe', gi))
                    sb = gi % 2
                    EE.wait_mark('se', (tag + 'pe', gi), 'pe')
                    if gi == 0:
                        EE.wait_mark('se', 'consts', 'dx')
                    if gi >= 2:
                        EE.wait_mark('se', (tag + 've', gi - 2), 've')
                    EE.act(t_.exs[sb][:, :], pb, AF.Exp, bias=bias_sb[ofc][:, :])
                    EE.act(t_.rs_[sb][:, :], pb, AF.Relu, bias=bias_sb[ofc][:, :],
                           inc=1, fenced=True)
                    EE.mk((tag + 'se', gi))
                    EE.wait_mark('ve', (tag + 'se', gi), 'se')
                    EE.ve_stt(dst[:, ofc, tb * 512:(tb + 1) * 512],
                              t_.exs[sb][:, :], 1.0, t_.rs_[sb][:, :],
                              ALU.min, ALU.add, inc=1, fenced=True)
                    EE.mk((tag + 've', gi))
                EE.mk(('rel_' + tag, ofc))

        proj_fm('k', dr.wk, t_.bkc, t_.kf)
        EE.phase('p2')
        if stop_after == 'p2':
            finish()
            return

        EE.wait_mark('pe', 'consts', 'dx')
        for h in range(H):
            ps = t_.pb[4 + h % 2].ap()[:, 0:129]
            for n in range(NCH):
                k2 = h * NCH + n
                pt = t_.pbf(6 + k2 % 2)
                if k2 >= 2:
                    EE.wait_mark('pe', ('ktmev', k2 - 2), 'se')
                EE.trans(pt, t_.kf[:, h, n * 128:(n + 1) * 128], inc=1)
                EE.mk(('ktmT', k2))
                EE.wait_mark('se', ('ktmT', k2), 'pe')
                if k2 >= 2:
                    EE.wait_mark('se', ('kvmm', k2 - 2), 'pe')
                EE.act(t_.ktm[k2 % 2][:, :], pt, AF.Copy, inc=1, fenced=True)
                EE.mk(('ktmev', k2))
                EE.wait_mark('pe', ('ktmev', k2), 'se')
                if n == 0 and h >= 2:
                    EE.wait_mark('pe', ('stotev', h - 2), 'se')
                EE.mm(ps, t_.ktm[k2 % 2][:, :], t_.v[n][:, h, :],
                      n == 0, n == NCH - 1, inc=1)
                EE.mk(('kvmm', k2))
            EE.mk(('stotmm', h))
            EE.wait_mark('se', ('stotmm', h), 'pe')
            EE.act(t_.S_tot[:, h, :], ps, AF.Copy, inc=1, fenced=True)
            EE.mk(('stotev', h))

        EE.wait_mark('sy', ('stotev', H - 1), 'se')
        EE.dma('sy', dr.cc_in.ap().rearrange("(h p) e -> p h e", p=128),
               t_.S_tot[:, :, :])
        EE.mk('ccin')

        proj_fm('q', dr.wq, t_.bqc, t_.qf)
        EE.wait_mark('gp', 'ccin', 'dx')
        EE.collective([dr.cc_in.ap().opt()], [dr.cc_out.ap().opt()])
        EE.wait('gp', 'cc', 1)

        EE.wait('sy', 'cc', 1)
        for j in range(NCORE):
            if j >= 1:
                EE.wait_mark('sy', ('pref', j - 1), 've')
            EE.dma('sy', t_.blk[:, :, :],
                   dr.cc_out.ap()[j * D:(j + 1) * D, :].rearrange("(h p) e -> p h e", p=128))
            EE.mk(('blkld', j))
            EE.wait_mark('ve', ('blkld', j), 'dx')
            if j == 0:
                EE.wait_mark('ve', 'consts', 'dx')
            EE.ve_stt(t_.S_run[:, :, :], t_.blk[:, :, :], t_.pmask[:, j:j + 1],
                      t_.S_run[:, :, :], ALU.mult, ALU.add, inc=1)
            EE.ve_drain()
            EE.mk(('pref', j))
        EE.wait_mark('se', ('pref', NCORE - 1), 've')
        EE.act(t_.S_bf[:, :, :], t_.S_run[:, :, :], AF.Copy, inc=1, fenced=True)
        EE.mk('sbf_init')
        if 'sbf_snap' in DEBUG_TAPS:
            EE.act(t_.blk[:, :, :], t_.S_bf[:, :, :], AF.Copy, inc=1)
            EE.mk('sbf_snap')
        EE.phase('p3')
        if stop_after == 'p3':
            finish()
            return

        import os as _os2
        _mxser = _os2.environ.get('KMX_SER', '0')
        _noscp = _os2.environ.get('KMX_NOSCP', '0') == '1'
        for n in range(NCH):
            for h in range(H):
                k = n * H + h
                if _mxser == '1':
                    EE.phase(('mxs', k))
                elif _mxser == '2' and h == 0:
                    EE.phase(('mxs', k))
                elif _mxser in ('3', '4', '5') and h == 0:
                    tgt = {'3': 'pe', '4': 've', '5': 'se'}[_mxser]
                    for eng2 in ('pe', 've', 'se'):
                        EE.fence(eng2)
                    EE.mk(('mxs', k))
                    for oth in ('pe', 've', 'se'):
                        if oth != tgt:
                            EE.wait_mark(tgt, ('mxs', k), oth)
                psA = t_.pb[k % 2].ap()[:, 0:128]
                psO = t_.pb[2 + k % 2].ap()[:, 0:129]
                psK = t_.pb[4].ap()[:, 0:129]
                ptk = t_.pbf(5 + k % 2)
                pta = t_.pbf(7)
                rc = t_.rcs[k % 2][:, :]
                if k >= 2:
                    EE.wait_mark('pe', ('mx_ktmev', k - 2), 'se')
                EE.trans(ptk, t_.kf[:, h, n * 128:(n + 1) * 128], inc=1)
                EE.mk(('mx_ktmT', k))
                if k >= 2:
                    EE.wait_mark('pe', ('mx_mask', k - 2), 've')
                EE.mm(psA, t_.kf[:, h, n * 128:(n + 1) * 128],
                      t_.qf[:, h, n * 128:(n + 1) * 128], True, True, inc=1)
                EE.mk(('mx_at', k))
                if n == 0 or _noscp:
                    EE.wait_mark('pe', 'sbf_init', 'se')
                else:
                    EE.wait_mark('pe', ('mx_scp', (n - 1) * H + h), 'se')
                if k >= 2:
                    EE.wait_mark('pe', ('mx_aosc', k - 2), 'se')
                    EE.wait_mark('pe', ('mx_rc', k - 2), 've')
                EE.mm(psO, t_.qf[:, h, n * 128:(n + 1) * 128], t_.S_bf[:, h, :],
                      True, False)
                EE.wait_mark('se', ('mx_ktmT', k), 'pe')
                if k >= 2:
                    EE.wait_mark('se', ('mx_kv', k - 2), 'pe')
                EE.act(t_.ktm[k % 2][:, :], ptk, AF.Copy, inc=1, fenced=True)
                EE.mk(('mx_ktmev', k))
                EE.wait_mark('ve', ('mx_at', k), 'pe')
                if k >= 2:
                    EE.wait_mark('ve', ('mx_aoT', k - 2), 'pe')
                if k == 0:
                    EE.wait_mark('ve', 'consts', 'dx')
                EE.ve_tt(t_.aosc[k % 2][:, :], psA, t_.mask[:, :], ALU.mult, inc=1, fenced=True)
                EE.mk(('mx_mask', k))
                EE.wait_mark('pe', ('mx_mask', k), 've')
                EE.mm(psO, t_.aosc[k % 2][:, :], t_.v[n][:, h, :], False, True, inc=1)
                EE.mk(('mx_intra', k))
                EE.wait_mark('pe', ('mx_ktmev', k), 'se')
                if k >= 1:
                    EE.wait_mark('pe', ('mx_sadd', k - 1), 've')
                EE.mm(psK, t_.ktm[k % 2][:, :], t_.v[n][:, h, :], True, True, inc=1)
                EE.mk(('mx_kv', k))
                EE.wait_mark('ve', ('mx_intra', k), 'pe')
                EE.ve_ts(rc, psO[:, 128:129], EPS_ATTN, None, ALU.add)
                EE.ve_drain()
                EE.ve_recip(rc, rc, inc=1, fenced=True)
                EE.mk(('mx_rc', k))
                EE.wait_mark('ve', ('mx_kv', k), 'pe')
                if n >= 1:
                    EE.wait_mark('ve', ('mx_scp', (n - 1) * H + h), 'se')
                EE.ve_tt(t_.S_run[:, h, :], t_.S_run[:, h, :], psK, ALU.add, inc=1, fenced=True)
                EE.mk(('mx_sadd', k))
                EE.wait_mark('se', ('mx_rc', k), 've')
                EE.wait_mark('se', ('mx_mask', k), 've')
                EE.act(t_.aosc[k % 2][:, :], psO[:, 0:128], AF.Copy, scale=rc, inc=1, fenced=True)
                EE.mk(('mx_aosc', k))
                EE.wait_mark('se', ('mx_sadd', k), 've')
                if _noscp:
                    EE.act(t_.blk[:, h, :], t_.S_run[:, h, :], AF.Copy, inc=1, fenced=True)
                else:
                    EE.act(t_.S_bf[:, h, :], t_.S_run[:, h, :], AF.Copy, inc=1, fenced=True)
                EE.mk(('mx_scp', k))
                EE.wait_mark('pe', ('mx_aosc', k), 'se')
                if k >= 1:
                    EE.wait_mark('pe', ('mx_aoev', k - 1), 'se')
                EE.trans(pta, t_.aosc[k % 2][:, :], inc=1)
                EE.mk(('mx_aoT', k))
                EE.wait_mark('se', ('mx_aoT', k), 'pe')
                EE.act(t_.ao_fm[:, h, n * 128:(n + 1) * 128], pta, AF.Copy, inc=1,
                       fenced=True)
                EE.mk(('mx_aoev', k))
        EE.phase('p4')
        if stop_after == 'p4':
            finish()
            return

        for tau in range(NCH):
            for s_ in ('pe', 've', 'se'):
                EE.wait_mark('sy', ('phase', 'p4'), s_)
            EE.dma('sy', t_.y[tau], dr.xr.ap()[tau * 128:(tau + 1) * 128, :])
            EE.mk(('xrld', tau))
        load_chunk(('bor',), t_.bor[:, :], dr.bor[:, :], ('rel_v', 7))
        for j in range(8):
            load_chunk(('wo', j), t_.vring[j % 2][:, :, :], wcol256(dr.wo, j),
                       ('rel_wo', j - 2) if j >= 2 else ('rel_v', 7))
        for j in range(8):
            pe_wait_chunk(('wo', j), ('wo', j + 1) if j < 7 else None)
            for tau in range(NCH):
                gi = j * 8 + tau
                pb = t_.pb[gi % 4].ap()[:, 0:256]
                if gi >= 4:
                    EE.wait_mark('pe', ('woev', gi - 4), 've')
                EE.mm(pb, t_.ones[:, :], t_.bor[:, j * 256:(j + 1) * 256], True, False)
                for hidc in range(DC):
                    EE.mm(pb, t_.ao_fm[:, hidc, tau * 128:(tau + 1) * 128],
                          t_.vring[j % 2][:, hidc, :], False, hidc == DC - 1,
                          inc=1 if hidc == DC - 1 else 0)
                EE.mk(('wope', gi))
                EE.wait_mark('ve', ('wope', gi), 'pe')
                EE.wait_mark('ve', ('xrld', tau), 'dx')
                EE.ve_tt(t_.y[tau][:, j * 256:(j + 1) * 256], pb,
                         t_.y[tau][:, j * 256:(j + 1) * 256], ALU.add, inc=1)
                EE.mk(('woev', gi))
            EE.mk(('rel_wo', j))
        EE.mk('wo_done')

        def layernorm(tag, tau, make_xfm):
            yv = t_.y[tau]
            EE.ve_red(t_.st1[:, :], yv, ALU.add, inc=1)
            EE.mk((tag + 's1', tau))
            EE.wait_mark('se', (tag + 's1', tau), 've')
            EE.act(t_.sq[:, :], yv, AF.Square, accum_out=t_.ssq[:, :], inc=1,
                   fenced=True)
            EE.mk((tag + 'ssq', tau))
            EE.ve_drain()
            EE.ve_ts(t_.mean[:, :], t_.st1[:, :], 1.0 / D, None, ALU.mult)
            EE.ve_drain()
            EE.ve_tt(t_.w_[:, :], t_.mean[:, :], t_.mean[:, :], ALU.mult)
            EE.wait_mark('ve', (tag + 'ssq', tau), 'se')
            EE.ve_drain()
            EE.ve_stt(t_.w_[:, :], t_.ssq[:, :], 1.0 / D, t_.w_[:, :],
                      ALU.mult, ALU.subtract)
            EE.ve_drain()
            EE.ve_ts(t_.w_[:, :], t_.w_[:, :], EPS_LN, None, ALU.add)
            EE.ve_drain()
            EE.ve_recip(t_.w_[:, :], t_.w_[:, :], inc=1, fenced=True)
            EE.mk((tag + 'rs', tau))
            EE.wait_mark('se', (tag + 'rs', tau), 've')
            EE.act(t_.rstd[:, :], t_.w_[:, :], AF.Sqrt, inc=1, fenced=True)
            EE.mk((tag + 'rstd', tau))
            EE.wait_mark('ve', (tag + 'rstd', tau), 'se')
            EE.ve_ts(yv, yv, t_.mean[:, :], t_.rstd[:, :],
                     ALU.subtract, ALU.mult, inc=1, fenced=True)
            EE.mk((tag + 'norm', tau))
            if make_xfm:
                EE.wait_mark('se', (tag + 'norm', tau), 've')
                if tau >= 1:
                    EE.wait_mark('se', (tag + 'x1T', (tau - 1) * DC + DC - 1), 'pe')
                EE.act(t_.xb[:, :], yv, AF.Copy, inc=1, fenced=True)
                EE.mk((tag + 'xb', tau))
                for dmc in range(DC):
                    kk = tau * DC + dmc
                    pt = t_.pbf(5 + kk % 2)
                    EE.wait_mark('pe', (tag + 'xb', tau), 'se')
                    if kk >= 2:
                        EE.wait_mark('pe', (tag + 'x1ev', kk - 2), 'se')
                    EE.trans(pt, t_.xb[:, dmc * 128:(dmc + 1) * 128], inc=1)
                    EE.mk((tag + 'x1T', kk))
                    EE.wait_mark('se', (tag + 'x1T', kk), 'pe')
                    EE.act(t_.x1_fm[:, dmc, tau * 128:(tau + 1) * 128], pt,
                           AF.Copy, inc=1, fenced=True)
                    EE.mk((tag + 'x1ev', kk))

        for tau in range(NCH):
            layernorm('ln1', tau, True)

        for tau in range(NCH):
            pbg = t_.pb[4].ap()[:, 0:8]
            EE.wait_mark('pe', ('ln1x1ev', tau * DC + DC - 1), 'se')
            EE.wait_mark('pe', 'smallw', 'dw')
            if tau >= 1:
                EE.wait_mark('pe', ('lgev', tau - 1), 'se')
            for dmc in range(DC):
                EE.mm(pbg, t_.x1_fm[:, dmc, tau * 128:(tau + 1) * 128],
                      t_.gt[:, dmc, :], dmc == 0, dmc == DC - 1,
                      inc=1 if dmc == DC - 1 else 0)
            EE.mk(('lgpe', tau))
            EE.wait_mark('se', ('lgpe', tau), 'pe')
            if tau >= 1:
                EE.wait_mark('se', ('comb', tau - 1), 've')
            EE.act(t_.lg[:, :], pbg, AF.Copy, inc=1, fenced=True)
            EE.mk(('lgev', tau))
            EE.wait_mark('ve', ('lgev', tau), 'se')
            EE.ve_red(t_.gm1[:, :], t_.lg[:, :], ALU.max, negate=True, inc=1,
                      fenced=True)
            EE.mk(('gmax', tau))
            EE.wait_mark('se', ('gmax', tau), 've')
            EE.act(t_.lg[:, :], t_.lg[:, :], AF.Exp, bias=t_.gm1[:, :], inc=1,
                   fenced=True)
            EE.mk(('gexp', tau))
            EE.wait_mark('ve', ('gexp', tau), 'se')
            cmb = t_.comb[:, tau * 8:(tau + 1) * 8]
            EE.ve_red(t_.gms[:, 0:1], t_.lg[:, :], ALU.add)
            EE.ve_drain()
            EE.ve_recip(t_.gms[:, 0:1], t_.gms[:, 0:1])
            EE.ve_drain()
            EE.ve_ts(cmb, t_.lg[:, :], t_.gms[:, 0:1], None, ALU.mult)
            EE.ve_drain()
            EE.ve_red(t_.gms[:, 1:2], cmb, ALU.max)
            EE.ve_drain()
            EE.ve_ts(t_.lg[:, :], cmb, t_.gms[:, 1:2], None, ALU.is_ge)
            EE.ve_drain()
            EE.ve_stt(t_.lg[:, :], t_.lg[:, :], -1e30, cmb, ALU.mult, ALU.add)
            EE.ve_drain()
            EE.ve_red(t_.gms[:, 2:3], t_.lg[:, :], ALU.max)
            EE.ve_drain()
            EE.ve_ts(t_.lg[:, :], cmb, t_.gms[:, 2:3], None, ALU.is_ge)
            EE.ve_drain()
            EE.ve_tt(cmb, cmb, t_.lg[:, :], ALU.mult, inc=1)
            EE.mk(('comb', tau))
        EE.phase('p5')
        if stop_after == 'p5':
            finish()
            return

        for ei, e in enumerate([-1] + list(range(8))):
            for icc in range(IC):
                for wi, we in enumerate((1, 3)):
                    ck = ei * 16 + icc * 2 + wi
                    load_chunk(('w13', ck), t_.wring[ck % 4][:, :, :],
                               w13_col(we, e, icc),
                               ('rel_w13', ck - 4) if ck >= 4 else ('rel_q', DC - 1))
            load_chunk(('w2', ei), t_.w2b[:, :, :], w2_all(e),
                       ('rel_eo', ei - 1) if ei >= 1 else 'wo_done')
            for icc in range(IC):
                for tb in range(2):
                    gi = ei * 16 + icc * 2 + tb
                    pbh = t_.pb[gi % 2].ap()[:, 0:512]
                    pbh3 = t_.pb[2 + gi % 2].ap()[:, 0:512]
                    if tb == 0:
                        nk = (('w13', ei * 16 + icc * 2 + 2) if icc < IC - 1
                              else ('w2', ei))
                        pe_wait_chunk(('w13', ei * 16 + icc * 2))
                        pe_wait_chunk(('w13', ei * 16 + icc * 2 + 1), nk)
                    if gi >= 2:
                        EE.wait_mark('pe', ('gsil', gi - 2), 'se')
                        EE.wait_mark('pe', ('gmul', gi - 2), 've')
                    for dmc in range(DC):
                        EE.mm(pbh, t_.wring[(ei * 16 + icc * 2) % 4][:, dmc, :],
                              t_.x1_fm[:, dmc, tb * 512:(tb + 1) * 512],
                              dmc == 0, dmc == DC - 1, inc=1 if dmc == DC - 1 else 0)
                    EE.mk(('ghpe', gi))
                    for dmc in range(DC):
                        EE.mm(pbh3, t_.wring[(ei * 16 + icc * 2 + 1) % 4][:, dmc, :],
                              t_.x1_fm[:, dmc, tb * 512:(tb + 1) * 512],
                              dmc == 0, dmc == DC - 1, inc=1 if dmc == DC - 1 else 0)
                    EE.mk(('gh3pe', gi))
                    EE.wait_mark('se', ('ghpe', gi), 'pe')
                    if gi >= 2:
                        EE.wait_mark('se', ('gmul', gi - 2), 've')
                    EE.act(t_.exs[gi % 2][:, :], pbh, AF.Silu, inc=1, fenced=True)
                    EE.mk(('gsil', gi))
                    EE.wait_mark('ve', ('gsil', gi), 'se')
                    EE.wait_mark('ve', ('gh3pe', gi), 'pe')
                    EE.ve_tt(t_.g[:, icc, tb * 512:(tb + 1) * 512],
                             t_.exs[gi % 2][:, :], pbh3, ALU.mult, inc=1, fenced=True)
                    EE.mk(('gmul', gi))
                EE.mk(('rel_w13', ei * 16 + icc * 2))
                EE.mk(('rel_w13', ei * 16 + icc * 2 + 1))
            pe_wait_chunk(('w2', ei),
                          ('w13', (ei + 1) * 16) if ei < 8 else None)
            for tau in range(NCH):
                for oc in range(4):
                    gi = ei * 32 + tau * 4 + oc
                    pbe = t_.pb[4 + gi % 4].ap()[:, 0:512]
                    if gi >= 4:
                        EE.wait_mark('pe', ('eoev', gi - 4), 've')
                    if tau == 0 and oc == 0:
                        EE.wait_mark('pe', ('gmul', ei * 16 + 15), 've')
                    for icc in range(IC):
                        EE.mm(pbe, t_.g[:, icc, tau * 128:(tau + 1) * 128],
                              t_.w2b[:, icc, oc * 512:(oc + 1) * 512],
                              icc == 0, icc == IC - 1, inc=1 if icc == IC - 1 else 0)
                    EE.mk(('eope', gi))
                    EE.wait_mark('ve', ('eope', gi), 'pe')
                    ysl = t_.y[tau][:, oc * 512:(oc + 1) * 512]
                    if e < 0:
                        EE.ve_tt(ysl, pbe, ysl, ALU.add, inc=1)
                    else:
                        EE.ve_stt(ysl, pbe, t_.comb[:, tau * 8 + e:tau * 8 + e + 1],
                                  ysl, ALU.mult, ALU.add, inc=1)
                    EE.mk(('eoev', gi))
            EE.mk(('rel_eo', ei))
        EE.phase('p6')
        if stop_after == 'p6':
            finish()
            return

        EE.mk('gp_end2')
        for tau in range(NCH):
            layernorm('ln2', tau, False)
            EE.wait_mark('sy', ('ln2norm', tau), 've')
            EE.dma('sy', dr.out.ap()[tau * 128:(tau + 1) * 128, :], t_.y[tau],
                   sem='so')
            EE.mk(('outst', tau))
        finish()

    marks = {}
    e0 = E(nc, t, sems, None, marks)
    program(e0)
    finals = [dict(e0.n)]

    with nc.Block() as block:
        @block.tensor
        def _(_e):
            ee = E(nc, t, sems, 'pe', marks)
            program(ee)
            finals.append(dict(ee.n))

        @block.vector
        def _(_e):
            ee = E(nc, t, sems, 've', marks)
            program(ee)
            finals.append(dict(ee.n))

        @block.scalar
        def _(_e):
            ee = E(nc, t, sems, 'se', marks)
            program(ee)
            finals.append(dict(ee.n))

        @block.sync
        def _(_e):
            ee = E(nc, t, sems, 'sy', marks)
            program(ee)
            finals.append(dict(ee.n))

        @block.gpsimd
        def _(_e):
            ee = E(nc, t, sems, 'gp', marks)
            program(ee)
            finals.append(dict(ee.n))
    for f in finals[1:]:
        assert f == finals[0], f"pass counter divergence: {f} vs {finals[0]}"

    return nc


_CACHE = {}


def _prep_inputs(inputs):
    x = np.asarray(inputs["L0_attn_x"], dtype=np.float32).reshape(8192, 2048)
    xT = np.ascontiguousarray(x.T)
    wq = np.ascontiguousarray(np.asarray(inputs["L0_attn_Wq"], np.float32))
    wk = np.ascontiguousarray(np.asarray(inputs["L0_attn_Wk"], np.float32))
    wv = np.ascontiguousarray(np.asarray(inputs["L0_attn_Wv"], np.float32))
    wo = np.ascontiguousarray(np.asarray(inputs["L0_attn_Wo"], np.float32))
    bq = np.asarray(inputs["L0_attn_bq"], np.float32)
    bk = np.asarray(inputs["L0_attn_bk"], np.float32)
    bv = np.asarray(inputs["L0_attn_bv"], np.float32)
    bo = np.asarray(inputs["L0_attn_bo"], np.float32)
    w1t = np.ascontiguousarray(np.asarray(inputs["L0_ffn_w1"], np.float32).transpose(0, 2, 1))
    w3t = np.ascontiguousarray(np.asarray(inputs["L0_ffn_w3"], np.float32).transpose(0, 2, 1))
    w2t = np.ascontiguousarray(np.asarray(inputs["L0_ffn_w2"], np.float32).transpose(0, 2, 1))
    ws1t = np.ascontiguousarray(np.asarray(inputs["L0_ffn_ws1"], np.float32).T)
    ws3t = np.ascontiguousarray(np.asarray(inputs["L0_ffn_ws3"], np.float32).T)
    ws2t = np.ascontiguousarray(np.asarray(inputs["L0_ffn_ws2"], np.float32).T)
    gt = np.ascontiguousarray(np.asarray(inputs["L0_ffn_gate_w"], np.float32).T)
    ident = np.eye(128, dtype=ml_dtypes.bfloat16)
    kk, cc_ = np.meshgrid(np.arange(128), np.arange(128), indexing='ij')
    maskat = (kk <= cc_).astype(np.float32)
    bqp = np.ascontiguousarray(bq.reshape(16, 128))
    bkp = np.ascontiguousarray(bk.reshape(16, 128))
    in_maps = []
    for i in range(NCORE):
        pm = np.zeros((128, 8), np.float32)
        pm[:, :i] = 1.0
        in_maps.append({
            "xT": np.ascontiguousarray(xT[:, i * T:(i + 1) * T]),
            "xr": np.ascontiguousarray(x[i * T:(i + 1) * T]),
            "wq": wq, "wk": wk, "wv": wv, "wo": wo,
            "bqp": bqp, "bkp": bkp,
            "bvr": bv.reshape(1, 2048), "bor": bo.reshape(1, 2048),
            "w1t": w1t, "w3t": w3t, "w2t": w2t,
            "ws1t": ws1t, "ws3t": ws3t, "ws2t": ws2t,
            "gt": gt, "identin": ident, "maskin": maskat, "pmask": pm,
        })
    return in_maps


def kernel(**inputs):
    if 'nc' not in _CACHE:
        _CACHE['nc'] = build_nc()
    nc = _CACHE['nc']
    in_maps = _prep_inputs(inputs)
    res = run_bass_kernel_spmd(nc, in_maps, core_ids=list(range(NCORE)))
    out = np.concatenate([res.results[i]["out"] for i in range(NCORE)], axis=0)
    return out.reshape(1, 8192, 2048)
